# version 01
# baseline (speedup 1.0000x reference)
import sys
sys.path.insert(0, "/opt/trn_rl_repo")

import numpy as np
import ml_dtypes
import concourse.bass as bass
import concourse.mybir as mybir
from concourse.bass_utils import run_bass_kernel_spmd

F32 = mybir.dt.float32
BF16 = mybir.dt.bfloat16
AF = mybir.ActivationFunctionType
ALU = mybir.AluOpType
AX = mybir.AxisListType
BF = ml_dtypes.bfloat16

D, DC, T, S, H = 2048, 16, 1024, 2048, 16
NKT = S // 128
I_MOE, IC, NE = 1408, 11, 15
EPS_LN = 1e-5
EPS_RMS = 1e-6
ENGINES = ("tensor", "scalar", "vector", "sync", "gpsimd")

SEMS = {
    "mm": "tensor", "sc": "scalar", "ve": "vector", "vf": "vector",
    "cc": "gpsimd", "st": "sync",
}
QSEM = {"gpsimd": ("w", 4), "sync": ("ld", 4), "scalar": ("sd", 2)}
for _q, (_pfx, _n) in QSEM.items():
    for _i in range(_n):
        SEMS[f"{_pfx}{_i}"] = _q


class P:

    def __init__(self):
        self.nc = bass.Bass("TRN2", target_bir_lowering=False)
        self.ops = {e: [] for e in ENGINES}
        self.cnt = {s: 0 for s in SEMS}
        self.inputs = {}
        self.outputs = {}
        self.dma_idx = {"gpsimd": 0, "sync": 0, "scalar": 0}
        self.d = {}
        self._sb = {}
        self._ps = {}

    def din(self, name, shape, dtype=F32):
        if name not in self.d:
            self.inputs[name] = (tuple(shape), dtype)
            self.d[name] = self.nc.dram_tensor(name, list(shape), dtype, kind="ExternalInput").ap()
        return self.d[name]

    def dout(self, name, shape, dtype=F32):
        if name not in self.d:
            self.outputs[name] = (tuple(shape), dtype)
            self.d[name] = self.nc.dram_tensor(name, list(shape), dtype, kind="ExternalOutput").ap()
        return self.d[name]

    def dint(self, name, shape, dtype=BF16):
        if name not in self.d:
            self.d[name] = self.nc.dram_tensor(name, list(shape), dtype, kind="Internal").ap()
        return self.d[name]

    def sb(self, name, shape, dtype):
        if name not in self._sb:
            self._sb[name] = self.nc.alloc_sbuf_tensor("sb_" + name, list(shape), dtype)
        return self._sb[name].ap()

    def psum(self, name, shape, dtype=F32):
        if name not in self._ps:
            self._ps[name] = self.nc.alloc_psum_tensor(name, list(shape), dtype)
        return self._ps[name].ap()

    def add(self, eng, fn, waits=(), inc=None, inc_by=1):
        ws = [(s, v) for (s, v) in waits if v and v > 0]
        h = None
        if inc is not None:
            assert SEMS[inc] == eng, f"sem {inc} owned by {SEMS[inc]}, emitted on {eng}"
            self.cnt[inc] += inc_by
            h = (inc, inc_by)
        self.ops[eng].append((fn, ws, h))
        return self.cnt[inc] if inc else None

    def c(self, sem):
        return self.cnt[sem]

    def dma(self, eng, fn, waits=()):
        pfx, n = QSEM[eng]
        k = self.dma_idx[eng]
        self.dma_idx[eng] += 1
        sem = f"{pfx}{k % n}"
        gate = list(waits)
        if k >= n:
            gate.append((sem, 16 * (k // n)))
        self.add(eng, fn, waits=gate, inc=sem, inc_by=16)
        return (sem, self.cnt[sem])

    def barrier_marks(self):
        return dict(self.cnt)

    def bw(self, marks, eng):
        return [(s, marks.get(s, 0)) for s, own in SEMS.items() if own != eng and marks.get(s, 0) > 0]

    def flush(self, block, sems):
        for eng_name in ENGINES:
            ops = self.ops[eng_name]
            if not ops:
                continue
            dec = getattr(block, eng_name)

            def make(ops_):
                def body(eng):
                    last_wait = {}
                    for fn, ws, h in ops_:
                        for s, v in ws:
                            if last_wait.get(s, -1) < v:
                                last_wait[s] = v
                                eng.wait_ge(sems[s], v)
                        ins = fn(eng)
                        if h is not None:
                            ins.then_inc(sems[h[0]], h[1])
                return body
            dec(make(ops))

    def build(self):
        from contextlib import ExitStack
        with ExitStack() as ctx:
            sems = {s: ctx.enter_context(self.nc.semaphore(f"s_{s}")) for s in SEMS}
            block = ctx.enter_context(self.nc.Block())
            self.flush(block, sems)
        return self.nc



def setup_common(p: P, need):
    p.xs = p.sb("xs", [128, DC, T], F32)
    p.xn = p.sb("xn", [128, DC, T], BF16)
    p.arena = p.sb("arena", [128, 46080], BF16)
    p.rows = p.sb("rows", [1, 3072], F32)
    p.ones_c = p.sb("ones_c", [128, 1], BF16)
    p.ones_rf = p.sb("ones_rf", [1, 128], F32)
    p.neg_rb = p.sb("neg_rb", [1, 128], BF16)
    p.biases = p.sb("biases", [128, H], F32)

    p.const_marks = []

    def ld(dst, src_name, shape, dtype=F32):
        ap = p.din(src_name, shape, dtype)
        p.const_marks.append(p.dma("sync", lambda e, d=dst, s=ap: e.dma_start(out=d, in_=s)))

    p.add("vector", lambda v: v.memset(p.ones_c[:], 1.0))
    p.add("vector", lambda v: v.memset(p.ones_rf[:], 1.0))
    p.add("vector", lambda v: v.memset(p.neg_rb[:], -1.0), inc="ve")
    if "moe" in need:
        p.idm = p.sb("idm", [128, 128], F32)
        ld(p.idm[:], "idm", [128, 128], F32)
        p.sel = p.sb("sel", [NE, NE * 128], BF16)
        ld(p.sel[:], "sel", [NE, NE * 128], BF16)
        p.cbh = p.sb("cbh", [128, T], BF16)
        p.add("vector", lambda v: v.memset(p.cbh[:], 0.5), inc="ve")
    if "attn1" in need:
        p.trig = p.sb("trig", [64, 2 * T], BF16)
        ld(p.trig[:], "trig", [64, 2 * T], BF16)
        p.rt = p.sb("rt", [64, 64], BF16)
        ld(p.rt[:], "rt", [64, 64], BF16)
        p.qgain = p.sb("qgain", [128, 6], F32)
        ld(p.qgain[:], "qgain", [128, 6], F32)
        p.kvgain = p.sb("kvgain", [128, 4], F32)
        ld(p.kvgain[:], "kvgain", [128, 4], F32)
    if "attn0" in need:
        p.qmrow = p.sb("qmrow", [1, T], F32)
        ld(p.qmrow[:], "qmrow", [1, T], F32)
    p.mark_consts_ve = p.c("ve")

    p.P0 = p.psum("P0", [128, 1024])
    p.P1 = p.psum("P1", [128, 1024])
    p.P2 = p.psum("P2", [128, 1024])
    p.P3 = p.psum("P3", [128, 1024])


def arena_v(p, off_b, shape, dtype):
    elems = int(np.prod(shape[1:]))
    if dtype == BF16:
        a = p.arena[0:shape[0], off_b // 2: off_b // 2 + elems]
    else:
        a = p.arena.bitcast(F32)[0:shape[0], off_b // 4: off_b // 4 + elems]
    if len(shape) == 2:
        return a
    if len(shape) == 3:
        return a.rearrange("p (a b) -> p a b", b=shape[-1])
    return a.rearrange("p (a b c) -> p a b c", b=shape[-2], c=shape[-1])


def load_x(p: P):
    x_d = p.din("x", [D, T], F32)
    p.dma("sync", lambda e: e.dma_start(out=p.xs[:], in_=x_d.rearrange("(c p) t -> p c t", p=128)))


def store_x(p: P):
    y_d = p.dout("y", [D, T], F32)
    m = p.barrier_marks()
    p.add("sync", lambda e: e.dma_start(out=y_d.rearrange("(c p) t -> p c t", p=128), in_=p.xs[:]),
          waits=p.bw(m, "sync"), inc="st", inc_by=16)
    st = p.c("st")
    p.add("sync", lambda e: e.nop(), waits=[("st", st)])



def emit_ln(p: P, quantize=False, a0=0):
    m = p.barrier_marks()
    sqt = arena_v(p, a0, [128, 2, T], BF16)
    tsc = arena_v(p, a0 + 4096, [128, 2, T], F32)
    mb = arena_v(p, a0 + 12288, [128, T], F32)
    rb = arena_v(p, a0 + 16384, [128, T], F32)
    if quantize:
        ts2 = arena_v(p, a0 + 20480, [128, 2, T], F32)
        q1t = arena_v(p, a0 + 28672, [128, T], BF16)
        q2t = arena_v(p, a0 + 30720, [128, T], BF16)

    r0 = p.rows[0:1, 0:1024]
    r1 = p.rows[0:1, 1024:2048]
    r2 = p.rows[0:1, 2048:3072]

    xb_done = p.add("vector", lambda v: v.tensor_copy(p.xn[:], p.xs[:]),
                    waits=p.bw(m, "vector"), inc="ve")
    p.add("tensor", lambda t: t.ldweights(p.ones_c[:]),
          waits=p.bw(m, "tensor") + [("ve", xb_done)])
    mm_chunk = {}
    for c in range(DC):
        sq_w = [("ve", xb_done)]
        if c >= 2:
            sq_w.append(("mm", mm_chunk[c - 2]))
        else:
            sq_w += p.bw(m, "scalar")
        sq_c = p.add("scalar",
                     lambda s, c=c: s.activation(out=sqt[:, c % 2, :], in_=p.xn[:, c, :], func=AF.Square),
                     waits=sq_w, inc="sc")
        for tt in range(2):
            p.add("tensor",
                  lambda t, c=c, tt=tt: t.matmul(p.P2[0:1, tt * 512:(tt + 1) * 512], p.ones_c[:],
                                                 p.xn[:, c, tt * 512:(tt + 1) * 512],
                                                 start=(c == 0), stop=(c == DC - 1)))
        for tt in range(2):
            mm_chunk[c] = p.add("tensor",
                  lambda t, c=c, tt=tt: t.matmul(p.P3[0:1, tt * 512:(tt + 1) * 512], p.ones_c[:],
                                                 sqt[:, c % 2, tt * 512:(tt + 1) * 512],
                                                 start=(c == 0), stop=(c == DC - 1)),
                  waits=[("sc", sq_c)] if tt == 0 else (), inc="mm")
    sums_mm = p.c("mm")
    f1 = p.add("vector", lambda v: v.tensor_scalar(r2, p.P2[0:1, :], 1.0 / D, None, ALU.mult),
               waits=[("mm", sums_mm)], inc="vf")
    f2 = p.add("vector", lambda v: v.tensor_scalar(r1, p.P3[0:1, :], 1.0 / D, None, ALU.mult), inc="vf")
    f3 = p.add("vector", lambda v: v.tensor_tensor(r0, r2, r2, ALU.mult), waits=[("vf", f2)], inc="vf")
    p.add("vector", lambda v: v.tensor_tensor(r1, r1, r0, ALU.subtract), waits=[("vf", f3)], inc="vf")
    f4a = p.c("vf")
    f4 = p.add("vector", lambda v: v.tensor_scalar(r1, r1, float(EPS_LN), None, ALU.add),
               waits=[("vf", f4a)], inc="ve")
    sc1 = p.add("scalar", lambda s: s.activation(out=r0, in_=r1, func=AF.Sqrt),
                waits=[("ve", f4)], inc="sc")
    f5 = p.add("vector", lambda v: v.reciprocal(r1, r0), waits=[("sc", sc1)], inc="ve")
    bc = None
    for i, (dst, src) in enumerate([(p.P0[:, 0:512], r2[0:1, 0:512]),
                                    (p.P0[:, 512:1024], r2[0:1, 512:1024]),
                                    (p.P1[:, 0:512], r1[0:1, 0:512]),
                                    (p.P1[:, 512:1024], r1[0:1, 512:1024])]):
        bc = p.add("tensor", lambda t, dst=dst, src=src: t.matmul(dst, p.ones_rf[:], src, start=True, stop=True),
                   waits=[("ve", f5)] if i == 0 else (), inc="mm")
    p.add("vector", lambda v: v.tensor_copy(mb[:], p.P0[:]), waits=[("mm", bc)], inc="vf")
    e2 = p.add("vector", lambda v: v.tensor_copy(rb[:], p.P1[:]), inc="ve")
    done = {}
    for c in range(DC):
        w0 = [("ve", e2)] if c == 0 else []
        p.add("vector", lambda v, c=c: v.tensor_tensor(tsc[:, c % 2, :], p.xs[:, c, :], mb[:], ALU.subtract),
              waits=w0, inc="vf")
        fsub = p.c("vf")
        if not quantize:
            done[c] = p.add("vector",
                            lambda v, c=c: v.tensor_tensor(p.xn[:, c, :], tsc[:, c % 2, :], rb[:], ALU.mult),
                            waits=[("vf", fsub)], inc="ve")
        else:
            p.add("vector", lambda v, c=c: v.tensor_tensor(ts2[:, c % 2, :], tsc[:, c % 2, :], rb[:], ALU.mult),
                  waits=[("vf", fsub)], inc="vf")
            fmul = p.c("vf")
            p.add("vector", lambda v, c=c: v.tensor_scalar(q1t[:], ts2[:, c % 2, :], 0.33, 0.5, ALU.is_gt, ALU.mult),
                  waits=[("vf", fmul)], inc="vf")
            p.add("vector", lambda v, c=c: v.tensor_scalar(q2t[:], ts2[:, c % 2, :], -0.33, -0.5, ALU.is_lt, ALU.mult),
                  inc="vf")
            fq = p.c("vf")
            done[c] = p.add("vector", lambda v, c=c: v.tensor_tensor(p.xn[:, c, :], q1t[:], q2t[:], ALU.add),
                            waits=[("vf", fq)], inc="ve")
    p.mark_xn = dict(done)
    p.mark_xn_all = p.c("ve")



def emit_ffn0(p: P):
    m0 = p.barrier_marks()
    emit_ln(p, quantize=True, a0=0)
    wT_d = p.din("wT", [D, D], F32)
    rT_d = p.din("rT", [D, D], BF16)

    A = 32768
    slW = [arena_v(p, A + i * 8192, [128, DC, 128], F32) for i in range(2)]
    slR = [arena_v(p, A + 16384 + i * 4096, [128, DC, 128], BF16) for i in range(2)]
    wr = [arena_v(p, A + 24576 + i * 4096, [128, DC, 128], BF16) for i in range(2)]
    c1t = arena_v(p, A + 32768, [128, DC, 128], BF16)
    c2t = arena_v(p, A + 36864, [128, DC, 128], BF16)

    psF = [p.P0[:, 0:512], p.P0[:, 512:1024], p.P1[:, 0:512], p.P1[:, 512:1024]]
    NS = 16

    q_mark, grp_mm, evac_ve = {}, {}, {}
    gidx = 0
    for s in range(NS):
        lw = p.bw(m0, "gpsimd") if s == 0 else []
        if s >= 2:
            lw = [("ve", q_mark[s - 2])]
        mW = p.dma("gpsimd", lambda g, s=s: g.dma_start(
            out=slW[s % 2][:], in_=wT_d.rearrange("(c p) o -> p c o", p=128)[:, :, s * 128:(s + 1) * 128]),
            waits=lw)
        mR = p.dma("gpsimd", lambda g, s=s: g.dma_start(
            out=slR[s % 2][:], in_=rT_d.rearrange("(c p) o -> p c o", p=128)[:, :, s * 128:(s + 1) * 128]))
        qw = [mW, mR]
        if s >= 2:
            qw.append(("mm", grp_mm[(s - 2, 1)]))
        p.add("vector", lambda v, s=s: v.tensor_scalar(c1t[:], slW[s % 2][:], 0.2, 0.6, ALU.is_gt, ALU.mult),
              waits=qw, inc="vf")
        p.add("vector", lambda v, s=s: v.tensor_scalar(c2t[:], slW[s % 2][:], -0.2, -0.6, ALU.is_lt, ALU.mult),
              inc="vf")
        fq = p.c("vf")
        p.add("vector", lambda v: v.tensor_tensor(c1t[:], c1t[:], c2t[:], ALU.add),
              waits=[("vf", fq)], inc="vf")
        fs = p.c("vf")
        q_mark[s] = p.add("vector", lambda v, s=s: v.tensor_tensor(wr[s % 2][:], c1t[:], slR[s % 2][:], ALU.add),
                          waits=[("vf", fs)], inc="ve")
        for tt in range(2):
            b = gidx % 4
            gw = [("ve", q_mark[s]), ("ve", p.mark_xn_all)]
            if gidx >= 4:
                gw.append(("ve", evac_ve[gidx - 4]))
            for dc in range(DC):
                p.add("tensor", lambda t, s=s, dc=dc: t.ldweights(wr[s % 2][:, dc, :]),
                      waits=gw if dc == 0 else ())
                grp_mm[(s, tt)] = p.add("tensor",
                    lambda t, s=s, tt=tt, dc=dc, b=b: t.matmul(
                        psF[b], wr[s % 2][:, dc, :], p.xn[:, dc, tt * 512:(tt + 1) * 512],
                        start=(dc == 0), stop=(dc == DC - 1)),
                    inc="mm" if dc == DC - 1 else None)
            evac_ve[gidx] = p.add("vector",
                lambda v, s=s, tt=tt, b=b: v.scalar_tensor_tensor(
                    out=p.xs[:, s, tt * 512:(tt + 1) * 512], in0=psF[b], scalar=0.5,
                    in1=p.xs[:, s, tt * 512:(tt + 1) * 512], op0=ALU.mult, op1=ALU.add),
                waits=[("mm", grp_mm[(s, tt)])], inc="ve")
            gidx += 1



def emit_moe(p: P):
    m0 = p.barrier_marks()
    emit_ln(p, quantize=False, a0=0)
    gwT_d = p.din("gwT", [D, NE], F32)
    w1T_d = p.din("w1T", [NE, D, I_MOE], F32)
    w3T_d = p.din("w3T", [NE, D, I_MOE], F32)
    w2T_d = p.din("w2T", [NE, I_MOE, D], F32)
    sw1T_d = p.din("sw1T", [D, I_MOE], F32)
    sw3T_d = p.din("sw3T", [D, I_MOE], F32)
    sw2T_d = p.din("sw2T", [I_MOE, D], F32)

    A = 20480
    gw = arena_v(p, A, [128, DC, NE], BF16)
    lgt = arena_v(p, A + 512, [128, 8, NE], F32)
    e_t = arena_v(p, A + 1024, [128, 8, NE], F32)
    prb = arena_v(p, A + 1536, [128, 8, NE], F32)
    meq = arena_v(p, A + 2048, [128, 8, NE], F32)
    tm2 = arena_v(p, A + 2560, [128, 8, NE], F32)
    cmb = arena_v(p, A + 3072, [128, 8, NE], F32)
    srows = arena_v(p, A + 3584, [128, 8, 4], F32)
    combT = arena_v(p, A + 3712, [15, T], BF16)
    cbt = arena_v(p, A + 5760, [128, 2, 2, 512], BF16)
    h1s = arena_v(p, A + 9856, [128, IC, T], BF16)
    tmpt = arena_v(p, A + 32384, [128, 2, 512], BF16)
    SL0 = A + 34432
    SLB = 12288
    slots = [arena_v(p, SL0 + i * SLB, [128, SLB // 2], BF16) for i in range(3)]
    assert SL0 + 3 * SLB <= 92160

    gw_ld = p.dma("gpsimd", lambda g: g.dma_start(out=gw[:], in_=gwT_d.rearrange("(c p) e -> p c e", p=128)),
                  waits=p.bw(m0, "gpsimd"))
    w_base_idx = p.dma_idx["gpsimd"]

    glog_mm, glog_evac = {}, {}
    for tc in range(8):
        bank = [p.P3[:, 0:512], p.P3[:, 512:1024]][tc % 2]
        for dc in range(DC):
            waits = []
            if dc == 0:
                waits = [gw_ld, ("ve", p.mark_xn_all)]
                if tc >= 2:
                    waits.append(("vf", glog_evac[tc - 2]))
            p.add("tensor", lambda t, tc=tc, dc=dc: t.ldweights(p.xn[:, dc, tc * 128:(tc + 1) * 128]),
                  waits=waits)
            glog_mm[tc] = p.add("tensor", lambda t, tc=tc, dc=dc, bank=bank: t.matmul(
                bank[0:128, 0:NE], p.xn[:, dc, tc * 128:(tc + 1) * 128], gw[:, dc, :],
                start=(dc == 0), stop=(dc == DC - 1)),
                inc="mm" if dc == DC - 1 else None)
        glog_evac[tc] = p.add("vector", lambda v, tc=tc, bank=bank: v.tensor_copy(lgt[:, tc, :], bank[0:128, 0:NE]),
                              waits=[("mm", glog_mm[tc])], inc="vf")

    comb_ve = {}
    for tc in range(8):
        rmax = srows[:, tc, 0:1]
        nrmax = srows[:, tc, 1:2]
        rrec = srows[:, tc, 2:3]
        m1 = srows[:, tc, 3:4]
        p.add("vector", lambda v, tc=tc, rmax=rmax: v.reduce_max(rmax, lgt[:, tc, :], axis=AX.X),
              waits=[("vf", glog_evac[tc])], inc="vf")
        f = p.c("vf")
        ex_r = p.add("vector", lambda v, rmax=rmax, nrmax=nrmax: v.tensor_scalar(nrmax, rmax, -1.0, None, ALU.mult),
                     waits=[("vf", f)], inc="ve")
        ex = p.add("scalar", lambda s, tc=tc, nrmax=nrmax: s.activation(out=e_t[:, tc, :], in_=lgt[:, tc, :],
                                                                       func=AF.Exp, bias=nrmax),
                   waits=[("ve", ex_r)], inc="sc")
        p.add("vector", lambda v, tc=tc, rrec=rrec: v.reduce_sum(rrec, e_t[:, tc, :], axis=AX.X),
              waits=[("sc", ex)], inc="vf")
        f = p.c("vf")
        p.add("vector", lambda v, rrec=rrec: v.reciprocal(rrec, rrec), waits=[("vf", f)], inc="vf")
        f = p.c("vf")
        p.add("vector", lambda v, tc=tc, rrec=rrec: v.tensor_scalar(prb[:, tc, :], e_t[:, tc, :], rrec, None, ALU.mult),
              waits=[("vf", f)], inc="vf")
        f = p.c("vf")
        p.add("vector", lambda v, tc=tc, m1=m1: v.reduce_max(m1, prb[:, tc, :], axis=AX.X),
              waits=[("vf", f)], inc="vf")
        f = p.c("vf")
        p.add("vector", lambda v, tc=tc, m1=m1: v.tensor_scalar(meq[:, tc, :], prb[:, tc, :], m1, None, ALU.is_ge),
              waits=[("vf", f)], inc="vf")
        f = p.c("vf")
        p.add("vector", lambda v, tc=tc: v.scalar_tensor_tensor(out=tm2[:, tc, :], in0=meq[:, tc, :], scalar=-2.0,
                                                                in1=prb[:, tc, :], op0=ALU.mult, op1=ALU.add),
              waits=[("vf", f)], inc="vf")
        f = p.c("vf")
        p.add("vector", lambda v, tc=tc, m1=m1: v.reduce_max(m1, tm2[:, tc, :], axis=AX.X),
              waits=[("vf", f)], inc="vf")
        f = p.c("vf")
        p.add("vector", lambda v, tc=tc, m1=m1: v.tensor_scalar(meq[:, tc, :], prb[:, tc, :], m1, None, ALU.is_ge),
              waits=[("vf", f)], inc="vf")
        f = p.c("vf")
        comb_ve[tc] = p.add("vector", lambda v, tc=tc: v.tensor_tensor(cmb[:, tc, :], prb[:, tc, :], meq[:, tc, :],
                                                                      ALU.mult),
                            waits=[("vf", f)], inc="ve")

    tr_mm, ct_sc = {}, {}
    for tc in range(8):
        bank = [p.P2.bitcast(F32), p.P2.bitcast(F32)][0]
        bview = [p.P2[0:NE, 0:128], p.P2[0:NE, 512:640]][tc % 2]
        waits = [("ve", comb_ve[tc])] + p.const_marks
        if tc >= 2:
            waits.append(("sc", ct_sc[tc - 2]))
        tr_mm[tc] = p.add("tensor", lambda t, tc=tc, bview=bview: t.matmul(
            bview, cmb[:, tc, :], p.idm[:], start=True, stop=True, is_transpose=True),
            waits=waits, inc="mm")
        ct_sc[tc] = p.add("scalar", lambda s, tc=tc, bview=bview: s.copy(
            out=combT[0:NE, tc * 128:(tc + 1) * 128], in_=bview),
            waits=[("mm", tr_mm[tc])], inc="sc")
    combT_done = p.c("sc")

    W1Q = ((0, 384), (384, 768), (768, 1152), (1152, 1408))
    W2Q = ((0, 512), (512, 1024), (1024, 1536), (1536, 2048))

    def quarters(e):
        qs = []
        qs += [((w1T_d[e] if e < NE else sw1T_d), lo, hi, DC) for lo, hi in W1Q]
        qs += [((w3T_d[e] if e < NE else sw3T_d), lo, hi, DC) for lo, hi in W1Q]
        qs += [((w2T_d[e] if e < NE else sw2T_d), lo, hi, IC) for lo, hi in W2Q]
        return qs

    def slot_view(slot, lo, hi, ncr):
        w = hi - lo
        return slots[slot][:, 0:ncr * w].rearrange("p (c f) -> p c f", f=w)

    IC_Q = []
    for ic in range(IC):
        col = ic * 128
        qi = min(col // 384, 3)
        IC_Q.append((qi, col - qi * 384))
    OC_Q = [(8 + oc // 4, (oc % 4) * 128) for oc in range(DC)]

    NEX = NE + 1
    h1_mm, h1_sc, h3_mm, g_ve, y_mm, y_ve = {}, {}, {}, {}, {}, {}
    qload_w, slot_free, cb_mm, cb_ve = {}, {}, {}, {}

    for e in range(NEX):
        if e < NE:
            p.add("tensor", lambda t, e=e: t.ldweights(p.sel[:, e * 128:(e + 1) * 128]),
                  waits=[("sc", combT_done)] + ([("ve", cb_ve[(e - 2, 1)])] if e >= 2 else []))
            for tt in range(2):
                cb_mm[(e, tt)] = p.add("tensor", lambda t, e=e, tt=tt: t.matmul(
                    [p.P3[:, 0:512], p.P3[:, 512:1024]][tt], p.sel[0:NE, e * 128:(e + 1) * 128],
                    combT[0:NE, tt * 512:(tt + 1) * 512], start=True, stop=True),
                    inc="mm")
                cb_ve[(e, tt)] = p.add("vector", lambda v, e=e, tt=tt: v.tensor_copy(
                    cbt[:, e % 2, tt, :], [p.P3[:, 0:512], p.P3[:, 512:1024]][tt]),
                    waits=[("mm", cb_mm[(e, tt)])], inc="ve")

        qdefs = quarters(e)
        for qi in range(12):
            idx = w_base_idx + e * 12 + qi
            qload_w[e * 12 + qi] = (f"w{idx % 4}", 16 * (idx // 4 + 1))

        gl = 0
        for ic in range(IC):
            for tt in range(2):
                qi, colb = IC_Q[ic]
                gq = e * 12 + qi
                bank = [p.P0[:, 0:512], p.P0[:, 512:1024]][gl % 2]
                gg = e * 2 * IC + gl
                waits = [qload_w[gq], ("ve", p.mark_xn_all)]
                if gg >= 2:
                    waits.append(("sc", h1_sc[gg - 2]))
                sv = slot_view(gq % 3, qdefs[qi][1], qdefs[qi][2], DC)
                for dc in range(DC):
                    p.add("tensor", lambda t, sv=sv, dc=dc, colb=colb: t.ldweights(sv[:, dc, colb:colb + 128]),
                          waits=waits if dc == 0 else ())
                    h1_mm[gg] = p.add("tensor", lambda t, sv=sv, dc=dc, colb=colb, tt=tt, bank=bank: t.matmul(
                        bank, sv[:, dc, colb:colb + 128], p.xn[:, dc, tt * 512:(tt + 1) * 512],
                        start=(dc == 0), stop=(dc == DC - 1)),
                        inc="mm" if dc == DC - 1 else None)
                h1_sc[gg] = p.add("scalar", lambda s, ic=ic, tt=tt, bank=bank: s.activation(
                    out=h1s[:, ic, tt * 512:(tt + 1) * 512], in_=bank, func=AF.Silu),
                    waits=[("mm", h1_mm[gg])], inc="sc")
                gl += 1
        for qi in range(4):
            last_ic = max(ic for ic in range(IC) if IC_Q[ic][0] == qi)
            slot_free[e * 12 + qi] = ("mm", h1_mm[e * 2 * IC + last_ic * 2 + 1])

        gl = 0
        for ic in range(IC):
            for tt in range(2):
                qi, colb = IC_Q[ic]
                gq = e * 12 + 4 + qi
                bank = [p.P1[:, 0:512], p.P1[:, 512:1024]][gl % 2]
                gg = e * 2 * IC + gl
                waits = [qload_w[gq]]
                if gg >= 2:
                    waits.append(("ve", g_ve[gg - 2]))
                sv = slot_view(gq % 3, qdefs[4 + qi][1], qdefs[4 + qi][2], DC)
                for dc in range(DC):
                    p.add("tensor", lambda t, sv=sv, dc=dc, colb=colb: t.ldweights(sv[:, dc, colb:colb + 128]),
                          waits=waits if dc == 0 else ())
                    h3_mm[gg] = p.add("tensor", lambda t, sv=sv, dc=dc, colb=colb, tt=tt, bank=bank: t.matmul(
                        bank, sv[:, dc, colb:colb + 128], p.xn[:, dc, tt * 512:(tt + 1) * 512],
                        start=(dc == 0), stop=(dc == DC - 1)),
                        inc="mm" if dc == DC - 1 else None)
                cb_ap = p.cbh[0:128, tt * 512:(tt + 1) * 512] if e == NE else cbt[:, e % 2, tt, :]
                gwts = [("mm", h3_mm[gg]), ("sc", h1_sc[gg])]
                if e < NE:
                    gwts.append(("ve", cb_ve[(e, tt)]))
                else:
                    gwts.append(("ve", p.mark_consts_ve))
                p.add("vector", lambda v, gl=gl, bank=bank, cb_ap=cb_ap: v.tensor_tensor(
                    tmpt[:, gl % 2, :], bank, cb_ap, ALU.mult),
                    waits=gwts, inc="vf")
                f = p.c("vf")
                g_ve[gg] = p.add("vector", lambda v, ic=ic, tt=tt, gl=gl: v.tensor_tensor(
                    h1s[:, ic, tt * 512:(tt + 1) * 512], tmpt[:, gl % 2, :],
                    h1s[:, ic, tt * 512:(tt + 1) * 512], ALU.mult),
                    waits=[("vf", f)], inc="ve")
                gl += 1
        for qi in range(4):
            last_ic = max(ic for ic in range(IC) if IC_Q[ic][0] == qi)
            slot_free[e * 12 + 4 + qi] = ("mm", h3_mm[e * 2 * IC + last_ic * 2 + 1])

        gl = 0
        for oc in range(DC):
            for tt in range(2):
                qi, colb = OC_Q[oc]
                gq = e * 12 + qi
                bank = [p.P2[:, 0:512], p.P2[:, 512:1024]][gl % 2]
                gg = e * 2 * DC + gl
                waits = [qload_w[gq], ("ve", g_ve[e * 2 * IC + (IC - 1) * 2 + tt])]
                if gg >= 2:
                    waits.append(("ve", y_ve[gg - 2]))
                sv = slot_view(gq % 3, qdefs[qi][1], qdefs[qi][2], IC)
                for ic in range(IC):
                    p.add("tensor", lambda t, sv=sv, ic=ic, colb=colb: t.ldweights(sv[:, ic, colb:colb + 128]),
                          waits=waits if ic == 0 else ())
                    y_mm[gg] = p.add("tensor", lambda t, sv=sv, ic=ic, colb=colb, tt=tt, bank=bank: t.matmul(
                        bank, sv[:, ic, colb:colb + 128], h1s[:, ic, tt * 512:(tt + 1) * 512],
                        start=(ic == 0), stop=(ic == IC - 1)),
                        inc="mm" if ic == IC - 1 else None)
                y_ve[gg] = p.add("vector", lambda v, oc=oc, tt=tt, bank=bank: v.tensor_tensor(
                    p.xs[:, oc, tt * 512:(tt + 1) * 512], p.xs[:, oc, tt * 512:(tt + 1) * 512],
                    bank, ALU.add),
                    waits=[("mm", y_mm[gg])], inc="ve")
                gl += 1
        for qi in range(8, 12):
            last_oc = max(oc for oc in range(DC) if OC_Q[oc][0] == qi)
            slot_free[e * 12 + qi] = ("mm", y_mm[e * 2 * DC + last_oc * 2 + 1])

        for qi, (dref, lo, hi, ncr) in enumerate(qdefs):
            gq = e * 12 + qi
            waits = []
            if gq >= 3:
                waits.append(slot_free[gq - 3])
            mk = p.dma("gpsimd", lambda g, dref=dref, lo=lo, hi=hi, ncr=ncr, gq=gq: g.dma_start(
                out=slot_view(gq % 3, lo, hi, ncr),
                in_=dref.rearrange("(c p) f -> p c f", p=128)[:, :, lo:hi]),
                waits=waits)
            assert mk == qload_w[gq], (mk, qload_w[gq])






CIN0_ROWS = 4097
SCALE0 = 1.0 / float(np.sqrt(128.0))


def emit_attn0(p: P):
    m0 = p.barrier_marks()
    emit_ln(p, quantize=False, a0=0)
    wq_d = p.din("Wq", [D, D], F32)
    wk_d = p.din("Wk", [D, D], F32)
    wv_d = p.din("Wv", [D, D], F32)
    wo_d = p.din("Wo", [D, D], F32)
    mask_d = p.din("mask0", [S, T], BF16)
    cin = p.dint("cin0", [CIN0_ROWS, T], BF16)
    cout = p.dint("cout0", [2 * CIN0_ROWS, T], BF16)

    wslQK = [arena_v(p, i * 8192, [128, DC, 256], BF16) for i in range(2)]
    kst = [arena_v(p, 16384 + i * 4096, [128, 2, T], BF16) for i in range(2)]
    qmb = arena_v(p, 24576, [128, T], BF16)
    sqh = arena_v(p, 26624, [128, T], BF16)
    wslV = [arena_v(p, 28672 + i * 8192, [128, DC, 256], BF16) for i in range(2)]
    Q = arena_v(p, 45056, [128, H, T], BF16)
    knrow = arena_v(p, 77824, [1, T], BF16)
    mrowb = arena_v(p, 79872, [1, H], BF16)
    vst = [arena_v(p, i * 4096, [128, 8, 256], BF16) for i in range(2)]
    r0 = p.rows[0:1, 0:1024]
    PQ = [p.P0[:, 0:512], p.P0[:, 512:1024], p.P1[:, 0:512], p.P1[:, 512:1024]]

    p.add("tensor", lambda t: t.matmul(p.P2[:, 0:512], p.ones_rf[:], p.qmrow[0:1, 0:512],
                                       start=True, stop=True),
          waits=[("ve", p.mark_xn_all)])
    qb_mm = p.add("tensor", lambda t: t.matmul(p.P2[:, 512:1024], p.ones_rf[:], p.qmrow[0:1, 512:1024],
                                               start=True, stop=True), inc="mm")
    qmb_ve = p.add("vector", lambda v: v.tensor_copy(qmb[:], p.P2[:]), waits=[("mm", qb_mm)], inc="ve")

    def wload(dram, s, slot_ap, guard, cols=256):
        return p.dma("gpsimd", lambda g, dram=dram, s=s, slot_ap=slot_ap, cols=cols: g.dma_start(
            out=slot_ap, in_=dram.rearrange("(c p) f -> p c f", p=128)[:, :, s * cols:(s + 1) * cols]),
            waits=guard)

    grp = 0
    pj = {}
    em = {}
    kcin_marks = {}
    vcin_marks = {}

    for s in range(8):
        guard = [("ve", p.mark_xn_all)] + (p.bw(m0, "gpsimd") if s == 0 else [])
        if s >= 2:
            guard = [pj[("q", s - 2, 3)]]
        wl = wload(wq_d, s, wslQK[s % 2][:], guard)
        for sub in range(2):
            hchunk = 2 * s + sub
            for tt in range(2):
                g4 = grp % 4
                gwaits = [wl, ("ve", p.mark_xn_all), qmb_ve and ("ve", qmb_ve)]
                if grp >= 4:
                    gwaits.append(em[grp - 4])
                for dc in range(DC):
                    p.add("tensor", lambda t, s=s, dc=dc, sub=sub: t.ldweights(
                        wslQK[s % 2][:, dc, sub * 128:(sub + 1) * 128]),
                        waits=gwaits if dc == 0 else ())
                    mmv = p.add("tensor",
                        lambda t, s=s, dc=dc, sub=sub, tt=tt, g4=g4: t.matmul(
                            PQ[g4], wslQK[s % 2][:, dc, sub * 128:(sub + 1) * 128],
                            p.xn[:, dc, tt * 512:(tt + 1) * 512],
                            start=(dc == 0), stop=(dc == DC - 1)),
                        inc="mm" if dc == DC - 1 else None)
                pj[("q", s, 2 * sub + tt)] = ("mm", mmv) if mmv else None
                ev = p.add("vector", lambda v, hchunk=hchunk, tt=tt, g4=g4: v.tensor_tensor(
                    Q[:, hchunk, tt * 512:(tt + 1) * 512], PQ[g4],
                    qmb[0:128, tt * 512:(tt + 1) * 512], ALU.mult),
                    waits=[("mm", mmv)], inc="ve")
                em[grp] = ("ve", ev)
                grp += 1
    kn_red = {}
    for s in range(8):
        guard = [pj[("q", 6 + (s % 2), 3)]] if s < 2 else [pj[("k", s - 2, 3)]]
        wl = wload(wk_d, s, wslQK[s % 2][:], guard)
        for sub in range(2):
            hh = 2 * s + sub
            for tt in range(2):
                g4 = grp % 4
                gwaits = [wl]
                if grp >= 4:
                    gwaits.append(em[grp - 4])
                for dc in range(DC):
                    p.add("tensor", lambda t, s=s, dc=dc, sub=sub: t.ldweights(
                        wslQK[s % 2][:, dc, sub * 128:(sub + 1) * 128]),
                        waits=gwaits if dc == 0 else ())
                    mmv = p.add("tensor",
                        lambda t, s=s, dc=dc, sub=sub, tt=tt, g4=g4: t.matmul(
                            PQ[g4], wslQK[s % 2][:, dc, sub * 128:(sub + 1) * 128],
                            p.xn[:, dc, tt * 512:(tt + 1) * 512],
                            start=(dc == 0), stop=(dc == DC - 1)),
                        inc="mm" if dc == DC - 1 else None)
                pj[("k", s, 2 * sub + tt)] = ("mm", mmv)
                kw2 = [("mm", mmv)]
                if s >= 2 and sub == 0 and tt == 0:
                    kw2.append(kcin_marks[s - 2])
                ev = p.add("scalar", lambda sc, s=s, sub=sub, tt=tt, g4=g4: sc.copy(
                    out=kst[s % 2][:, sub, tt * 512:(tt + 1) * 512], in_=PQ[g4]),
                    waits=kw2, inc="sc")
                em[grp] = ("sc", ev)
                grp += 1
            sq_sc = p.add("scalar", lambda sc, s=s, sub=sub: sc.activation(
                out=sqh[:], in_=kst[s % 2][:, sub, :], func=AF.Square),
                waits=[em[grp - 1], ("ve", kn_red.get(hh - 1, 0))], inc="sc")
            p.add("tensor", lambda t: t.ldweights(p.ones_c[:]),
                  waits=[("sc", sq_sc), ("ve", kn_red.get(hh - 1, 0) or qmb_ve)])
            kn_mm = None
            for tt in range(2):
                kn_mm = p.add("tensor", lambda t, tt=tt: t.matmul(
                    p.P2[0:1, tt * 512:(tt + 1) * 512], p.ones_c[:], sqh[0:128, tt * 512:(tt + 1) * 512],
                    start=True, stop=True),
                    inc="mm" if tt == 1 else None)
            kn_red[hh] = p.add("vector", lambda v, hh=hh: v.reduce_max(
                r0[0:1, hh:hh + 1], p.P2[0:1, 0:1024], axis=AX.X),
                waits=[("mm", kn_mm)], inc="ve")
        kcin_marks[s] = p.dma("sync", lambda e, s=s: e.dma_start(
            out=cin[256 * s:256 * (s + 1), :].rearrange("(c p) t -> p c t", p=128),
            in_=kst[s % 2][:]),
            waits=[em[grp - 1]])

    qn_red = {}
    for hh in range(H):
        prev = [("ve", kn_red[H - 1])] if hh == 0 else [("ve", qn_red[hh - 1])]
        sq_sc = p.add("scalar", lambda sc, hh=hh: sc.activation(
            out=sqh[:], in_=Q[:, hh, :], func=AF.Square), waits=prev, inc="sc")
        p.add("tensor", lambda t: t.ldweights(p.ones_c[:]), waits=[("sc", sq_sc)] + prev)
        qn_mm = None
        for tt in range(2):
            qn_mm = p.add("tensor", lambda t, tt=tt: t.matmul(
                p.P3[0:1, tt * 512:(tt + 1) * 512], p.ones_c[:], sqh[0:128, tt * 512:(tt + 1) * 512],
                start=True, stop=True), inc="mm" if tt == 1 else None)
        qn_red[hh] = p.add("vector", lambda v, hh=hh: v.reduce_max(
            r0[0:1, 16 + hh:17 + hh], p.P3[0:1, 0:1024], axis=AX.X),
            waits=[("mm", qn_mm)], inc="ve")
    p.add("vector", lambda v: v.memset(knrow[:], 0.0), waits=[("ve", kn_red[H - 1])], inc="vf")
    knz = p.c("vf")
    knr_ve = p.add("vector", lambda v: v.tensor_copy(knrow[0:1, 0:16], r0[0:1, 0:16]),
                   waits=[("vf", knz)], inc="ve")
    knrow_mark = p.dma("sync", lambda e: e.dma_start(out=cin[4096:4097, :], in_=knrow[:]),
                       waits=[("ve", knr_ve)])

    for s in range(8):
        guard = [pj[("k", 6 + (s % 2), 3)]] if s < 2 else [pj[("v", s - 2, 7)]]
        wl = wload(wv_d, s, wslV[s % 2][:], guard)
        for tc in range(8):
            g4 = grp % 4
            gwaits = [wl]
            if grp >= 4:
                gwaits.append(em[grp - 4])
            if s == 0 and tc == 0:
                gwaits.append(pj[("k", 7, 3)])
            for dc in range(DC):
                p.add("tensor", lambda t, dc=dc, tc=tc: t.ldweights(p.xn[:, dc, tc * 128:(tc + 1) * 128]),
                      waits=gwaits if dc == 0 else ())
                mmv = p.add("tensor", lambda t, s=s, dc=dc, tc=tc, g4=g4: t.matmul(
                    PQ[g4][0:128, 0:256],
                    p.xn[:, dc, tc * 128:(tc + 1) * 128], wslV[s % 2][:, dc, :],
                    start=(dc == 0), stop=(dc == DC - 1)),
                    inc="mm" if dc == DC - 1 else None)
            pj[("v", s, tc)] = ("mm", mmv)
            vw2 = [("mm", mmv)]
            if s >= 2 and tc == 0:
                vw2.append(vcin_marks[s - 2])
            ev = p.add("scalar", lambda sc, s=s, tc=tc, g4=g4: sc.copy(
                out=vst[s % 2][:, tc, :], in_=PQ[g4][0:128, 0:256]),
                waits=vw2, inc="sc")
            em[grp] = ("sc", ev)
            grp += 1
        vcin_marks[s] = p.dma("sync", lambda e, s=s: e.dma_start(
            out=cin[2048 + (s // 4) * 1024:3072 + (s // 4) * 1024,
                    (s % 4) * 256:(s % 4) * 256 + 256].rearrange("(tc p) f -> p tc f", p=128),
            in_=vst[s % 2][:]),
            waits=[em[grp - 1]])
    stageA_sc = em[grp - 1]

    allw = list(kcin_marks.values()) + list(vcin_marks.values()) + [knrow_mark]
    p.add("gpsimd", lambda g: g.collective_compute(
        "AllGather", ALU.bypass, replica_groups=[[0, 1], [2, 3], [4, 5], [6, 7]],
        ins=[cin.tensor.ap().opt()], outs=[cout.tensor.ap().opt()]),
        waits=allw, inc="cc")
    cc_mark = ("cc", p.c("cc"))

    khb = [arena_v(p, i * 4096, [128, S], BF16) for i in range(2)]
    vhb = [arena_v(p, 8192 + i * 4096, [128, NKT, 128], BF16) for i in range(2)]
    mskb = [arena_v(p, 16384 + i * 2048, [128, T], BF16) for i in range(2)]
    Pt = [arena_v(p, 20480 + i * 2048, [128, T], BF16) for i in range(3)]
    rbb = arena_v(p, 26624, [128, T], F32)
    knrk = arena_v(p, 30720, [1, 2, 16], BF16)
    O = p.xn.rearrange("p a b -> p a b")

    r1 = p.rows[0:1, 1024:2048]
    r2 = p.rows[0:1, 2048:3072]

    knm1 = p.dma("sync", lambda e: e.dma_start(out=knrk[0:1, 0, :], in_=cout[4096:4097, 0:16]),
                 waits=[cc_mark])
    knm2 = p.dma("sync", lambda e: e.dma_start(out=knrk[0:1, 1, :],
                                               in_=cout[CIN0_ROWS + 4096:CIN0_ROWS + 4097, 0:16]))
    f = p.add("vector", lambda v: v.tensor_tensor(r1[0:1, 0:16], knrk[0:1, 0, :], knrk[0:1, 1, :], ALU.max),
              waits=[knm1, knm2, ("ve", qn_red[H - 1])], inc="vf")
    f = p.add("vector", lambda v: v.tensor_tensor(r1[0:1, 16:32], r1[0:1, 0:16], r0[0:1, 16:32], ALU.mult),
              waits=[("vf", f)], inc="ve")
    sqb = p.add("scalar", lambda s: s.activation(out=mrowb[0:1, :], in_=r1[0:1, 16:32],
                                                 func=AF.Sqrt, scale=float(SCALE0 * SCALE0)),
                waits=[("ve", f)], inc="sc")
    bias_mm = None
    for hh in range(H):
        bias_mm = p.add("tensor", lambda t, hh=hh: t.matmul(
            p.P3[:, hh:hh + 1], p.neg_rb[:], mrowb[0:1, hh:hh + 1], start=True, stop=True),
            waits=[("sc", sqb), ("ve", qn_red[H - 1])] if hh == 0 else (),
            inc="mm" if hh == H - 1 else None)
    bias_ve = p.add("vector", lambda v: v.tensor_copy(p.biases[0:128, 0:H], p.P3[:, 0:H]),
                    waits=[("mm", bias_mm)], inc="ve")

    psP = [p.P0, p.P1]
    gm = [0]
    kv_ld = {}
    msk_ve = {}
    head_end_mm = {}
    o_ev = {}
    sums_f = {}

    def head_dma(h):
        hb = h % 2
        fh, colr = h // 8, (h % 8) * 128
        wv = []
        if h >= 2:
            wv = [head_end_mm[h - 2]]
        else:
            wv = [cc_mark]
        mks = []
        for r in range(2):
            mks.append(p.dma("sync", lambda e, h=h, r=r, hb=hb: e.dma_start(
                out=khb[hb][:, r * 1024:(r + 1) * 1024],
                in_=cout[r * CIN0_ROWS + h * 128:r * CIN0_ROWS + (h + 1) * 128, :]),
                waits=wv if r == 0 else ()))
            mks.append(p.dma("sync", lambda e, h=h, r=r, hb=hb, fh=fh, colr=colr: e.dma_start(
                out=vhb[hb][:, r * 8:(r + 1) * 8, :],
                in_=cout[r * CIN0_ROWS + 2048 + fh * 1024:r * CIN0_ROWS + 2048 + fh * 1024 + 1024,
                         colr:colr + 128].rearrange("(tc p) d -> p tc d", p=128))))
        kv_ld[h] = mks

    head_dma(0)
    head_dma(1)
    prev_exp = {}
    rbcopy_f = {}
    mmo_kt = {}
    for h in range(H):
        hb = h % 2
        exp_sc = {}
        mm_after_j = {}
        for j in range(NKT + 2):
            if j < NKT:
                g = gm[0]
                mw = [("ve", msk_ve[g - 2])] if g >= 2 else []
                msk_sd = p.dma("scalar", lambda sc, j=j, g=g: sc.dma_start(
                    out=mskb[g % 2][:], in_=mask_d[j * 128:(j + 1) * 128, :]),
                    waits=mw)
                pw = list(kv_ld[h]) + [("ve", bias_ve)]
                if j >= 2:
                    pw.append(("sc", exp_sc[j - 2]))
                elif h > 0:
                    pw.append(("sc", prev_exp[(h - 1, NKT - 2 + j)]))
                elif h == 0:
                    pw.append(stageA_sc)
                p.add("tensor", lambda t, h=h, j=j, hb=hb: t.ldweights(khb[hb][:, j * 128:(j + 1) * 128]),
                      waits=pw)
                mmv = None
                for qt in range(2):
                    mmv = p.add("tensor", lambda t, h=h, j=j, qt=qt, hb=hb: t.matmul(
                        psP[j % 2][:, qt * 512:(qt + 1) * 512], khb[hb][:, j * 128:(j + 1) * 128],
                        Q[:, h, qt * 512:(qt + 1) * 512], start=True, stop=True),
                        inc="mm" if qt == 1 else None)
                mm_after_j[j] = mmv
                ew = [("mm", mmv)]
                if j >= 3:
                    ew.append(("mm", mmo_kt[(h, j - 3)]))
                elif h > 0:
                    ew.append(("mm", mmo_kt[(h - 1, NKT - 3 + j)]))
                exp_sc[j] = p.add("scalar", lambda s, h=h, j=j: s.activation(
                    out=Pt[j % 3][:], in_=psP[j % 2][:], func=AF.Exp,
                    bias=p.biases[0:128, h:h + 1], scale=float(SCALE0)),
                    waits=ew, inc="sc")
                prev_exp[(h, j)] = exp_sc[j]
                msk_ve[g] = p.add("vector", lambda v, j=j, g=g: v.tensor_tensor(
                    Pt[j % 3][:], Pt[j % 3][:], mskb[g % 2][:], ALU.mult),
                    waits=[("sc", exp_sc[j]), msk_sd], inc="ve")
                gm[0] += 1
            if j >= 2:
                kt = j - 2
                sw = [("ve", msk_ve[h * NKT + kt])]
                if kt == 0 and h > 0:
                    sw.append(("vf", rbcopy_f[h - 1]))
                    sw.append(("ve", o_ev[h - 1]))
                p.add("tensor", lambda t: t.ldweights(p.ones_c[:]), waits=sw)
                for qt in range(2):
                    p.add("tensor", lambda t, kt=kt, qt=qt: t.matmul(
                        p.P2[0:1, qt * 512:(qt + 1) * 512], p.ones_c[:],
                        Pt[kt % 3][0:128, qt * 512:(qt + 1) * 512],
                        start=(kt == 0), stop=(kt == NKT - 1)))
                p.add("tensor", lambda t, kt=kt, hb=hb: t.ldweights(vhb[hb][:, kt, :]))
                mmo = None
                for qt in range(2):
                    mmo = p.add("tensor", lambda t, kt=kt, qt=qt, hb=hb: t.matmul(
                        p.P3[:, qt * 512:(qt + 1) * 512], vhb[hb][:, kt, :],
                        Pt[kt % 3][0:128, qt * 512:(qt + 1) * 512],
                        start=(kt == 0), stop=(kt == NKT - 1)),
                        inc="mm" if qt == 1 else None)
                mmo_kt[(h, kt)] = mmo
        head_end_mm[h] = ("mm", mmo_kt[(h, NKT - 1)])
        if h + 2 < H:
            head_dma(h + 2)
        f = p.add("vector", lambda v: v.tensor_scalar(r1, p.P2[0:1, 0:1024], 1e-30, None, ALU.max),
                  waits=[head_end_mm[h]], inc="vf")
        sums_f[h] = f
        f2 = p.add("vector", lambda v: v.reciprocal(r2, r1), waits=[("vf", f)], inc="ve")
        rb_mm = None
        for qt in range(2):
            rb_mm = p.add("tensor", lambda t, qt=qt: t.matmul(
                p.P2[:, qt * 512:(qt + 1) * 512], p.ones_rf[:], r2[0:1, qt * 512:(qt + 1) * 512],
                start=True, stop=True),
                waits=[("ve", f2)] if qt == 0 else (), inc="mm" if qt == 1 else None)
        f3 = p.add("vector", lambda v: v.tensor_copy(rbb[:], p.P2[:]), waits=[("mm", rb_mm)], inc="vf")
        rbcopy_f[h] = f3
        o_ev[h] = p.add("vector", lambda v, h=h: v.tensor_tensor(
            O[:, h, :], p.P3[:], rbb[:], ALU.mult),
            waits=[("vf", f3)], inc="ve")

    wslO = [arena_v(p, i * 8192, [128, DC, 256], BF16) for i in range(2)]
    PQ2 = [p.P0[:, 0:512], p.P0[:, 512:1024], p.P1[:, 0:512], p.P1[:, 512:1024]]
    grp2 = 0
    emo = {}
    pjo = {}
    for s in range(8):
        guard = [head_end_mm[H - 1], ("ve", o_ev[H - 1])] if s < 2 else [pjo[(s - 2, 3)]]
        wl = wload(wo_d, s, wslO[s % 2][:], guard)
        for sub in range(2):
            fc2 = 2 * s + sub
            for tt in range(2):
                g4 = grp2 % 4
                gwaits = [wl, ("ve", o_ev[H - 1])]
                if grp2 >= 4:
                    gwaits.append(emo[grp2 - 4])
                for fc in range(DC):
                    p.add("tensor", lambda t, s=s, fc=fc, sub=sub: t.ldweights(
                        wslO[s % 2][:, fc, sub * 128:(sub + 1) * 128]),
                        waits=gwaits if fc == 0 else ())
                    mmv = p.add("tensor", lambda t, s=s, fc=fc, sub=sub, tt=tt, g4=g4: t.matmul(
                        PQ2[g4], wslO[s % 2][:, fc, sub * 128:(sub + 1) * 128],
                        O[:, fc, tt * 512:(tt + 1) * 512],
                        start=(fc == 0), stop=(fc == DC - 1)),
                        inc="mm" if fc == DC - 1 else None)
                pjo[(s, 2 * sub + tt)] = ("mm", mmv)
                ev = p.add("vector", lambda v, fc2=fc2, tt=tt, g4=g4: v.tensor_tensor(
                    p.xs[:, fc2, tt * 512:(tt + 1) * 512], p.xs[:, fc2, tt * 512:(tt + 1) * 512],
                    PQ2[g4], ALU.add),
                    waits=[("mm", mmv)], inc="ve")
                emo[grp2] = ("ve", ev)
                grp2 += 1





CIN1_ROWS = 576
SCALE1 = 1.0 / float(np.sqrt(192.0))


def emit_attn1(p: P):
    m0 = p.barrier_marks()
    emit_ln(p, quantize=False, a0=0)
    wqa_d = p.din("w_qa", [D, 768], F32)
    wqb_d = p.din("w_qb", [768, 3072], F32)
    wkva_d = p.din("w_kva", [D, 576], F32)
    wkvb_d = p.din("w_kvb", [512, 4096], F32)
    wo_d = p.din("w_o", [D, D], F32)
    mask_d = p.din("mask1", [S, T], BF16)
    cin = p.dint("cin1", [CIN1_ROWS, T], BF16)
    cout = p.dint("cout1", [2 * CIN1_ROWS, T], BF16)

    krb = arena_v(p, 0, [64, T], BF16)
    kvn_loc = arena_v(p, 2048, [128, 4, T], BF16)
    kroped_loc = arena_v(p, 10240, [64, T], BF16)
    rb1 = arena_v(p, 12288, [128, T], F32)
    sqA = arena_v(p, 16384, [128, T], BF16)
    qa = arena_v(p, 20480, [128, 6, T], F32)
    ckv = arena_v(p, 45056, [128, 4, T], F32)
    wslA = [arena_v(p, 61440 + i * 8192, [128, DC, 256], BF16) for i in range(2)]
    q_an = arena_v(p, 79872, [128, 6, T], BF16)
    mb1 = arena_v(p, 78336, [1, H], BF16)

    PQ = [p.P0[:, 0:512], p.P0[:, 512:1024], p.P1[:, 0:512], p.P1[:, 512:1024]]
    r0 = p.rows[0:1, 0:1024]
    r1 = p.rows[0:1, 1024:2048]
    r2 = p.rows[0:1, 2048:3072]

    def wload(dram, s, slot_ap, guard, cols=256):
        return p.dma("gpsimd", lambda g, dram=dram, s=s, slot_ap=slot_ap, cols=cols: g.dma_start(
            out=slot_ap, in_=dram.rearrange("(c p) f -> p c f", p=128)[:, :, s * cols:(s + 1) * cols]),
            waits=guard)

    grp = 0
    em = {}
    pj = {}
    for s in range(3):
        guard = [("ve", p.mark_xn_all)] + (p.bw(m0, "gpsimd") if s == 0 else [])
        if s >= 2:
            guard = [pj[("qa", s - 2, 3)]]
        wl = wload(wqa_d, s, wslA[s % 2][:], guard)
        for sub in range(2):
            qc = 2 * s + sub
            for tt in range(2):
                g4 = grp % 4
                gwaits = [wl, ("ve", p.mark_xn_all)]
                if grp >= 4:
                    gwaits.append(em[grp - 4])
                for dc in range(DC):
                    p.add("tensor", lambda t, s=s, dc=dc, sub=sub: t.ldweights(
                        wslA[s % 2][:, dc, sub * 128:(sub + 1) * 128]),
                        waits=gwaits if dc == 0 else ())
                    mmv = p.add("tensor", lambda t, s=s, dc=dc, sub=sub, tt=tt, g4=g4: t.matmul(
                        PQ[g4], wslA[s % 2][:, dc, sub * 128:(sub + 1) * 128],
                        p.xn[:, dc, tt * 512:(tt + 1) * 512],
                        start=(dc == 0), stop=(dc == DC - 1)),
                        inc="mm" if dc == DC - 1 else None)
                pj[("qa", s, 2 * sub + tt)] = ("mm", mmv)
                ev = p.add("vector", lambda v, qc=qc, tt=tt, g4=g4: v.tensor_copy(
                    qa[:, qc, tt * 512:(tt + 1) * 512], PQ[g4]),
                    waits=[("mm", mmv)], inc="ve")
                em[grp] = ("ve", ev)
                grp += 1
    for s in range(2):
        guard = [pj[("qa", 2 - s, 3)]]
        wl = wload(wkva_d, s, wslA[s % 2][:], guard)
        for sub in range(2):
            kc = 2 * s + sub
            for tt in range(2):
                g4 = grp % 4
                gwaits = [wl]
                if grp >= 4:
                    gwaits.append(em[grp - 4])
                for dc in range(DC):
                    p.add("tensor", lambda t, s=s, dc=dc, sub=sub: t.ldweights(
                        wslA[s % 2][:, dc, sub * 128:(sub + 1) * 128]),
                        waits=gwaits if dc == 0 else ())
                    mmv = p.add("tensor", lambda t, s=s, dc=dc, sub=sub, tt=tt, g4=g4: t.matmul(
                        PQ[g4], wslA[s % 2][:, dc, sub * 128:(sub + 1) * 128],
                        p.xn[:, dc, tt * 512:(tt + 1) * 512],
                        start=(dc == 0), stop=(dc == DC - 1)),
                        inc="mm" if dc == DC - 1 else None)
                pj[("kv", s, 2 * sub + tt)] = ("mm", mmv)
                ev = p.add("vector", lambda v, kc=kc, tt=tt, g4=g4: v.tensor_copy(
                    ckv[:, kc, tt * 512:(tt + 1) * 512], PQ[g4]),
                    waits=[("mm", mmv)], inc="ve")
                em[grp] = ("ve", ev)
                grp += 1
    krs = arena_v(p, 61440, [128, DC, 64], BF16)
    krl = p.dma("gpsimd", lambda g: g.dma_start(
        out=krs[:], in_=wkva_d.rearrange("(c p) f -> p c f", p=128)[:, :, 512:576]),
        waits=[pj[("kv", 0, 3)], pj[("kv", 1, 3)]])
    kr_mm = None
    for tt in range(2):
        gwaits = [krl]
        if tt == 0:
            gwaits += [em[grp - 4], em[grp - 3]]
        for dc in range(DC):
            p.add("tensor", lambda t, dc=dc: t.ldweights(krs[:, dc, :]),
                  waits=gwaits if dc == 0 else ())
            kr_mm = p.add("tensor", lambda t, dc=dc, tt=tt: t.matmul(
                [p.P0[0:64, 0:512], p.P0[0:64, 512:1024]][tt], krs[:, dc, :],
                p.xn[:, dc, tt * 512:(tt + 1) * 512],
                start=(dc == 0), stop=(dc == DC - 1)),
                inc="mm" if dc == DC - 1 else None)
    krb_sc = p.add("scalar", lambda sc: sc.copy(out=krb[:], in_=p.P0[0:64, :]),
                   waits=[("mm", kr_mm)], inc="sc")

    def emit_rms(src_buf, nch, inv_n, gain, out_buf, dep_ve, dep_extra=()):
        mmr = None
        sq_marks = {}
        for c in range(nch):
            sw = [("ve", dep_ve)] + list(dep_extra)
            if c >= 1:
                sw.append(("mm", sq_marks[c - 1]))
            ssc = p.add("scalar", lambda s, c=c: s.activation(out=sqA[:], in_=src_buf[:, c, :],
                                                             func=AF.Square),
                        waits=sw, inc="sc")
            p.add("tensor", lambda t: t.ldweights(p.ones_c[:]), waits=[("sc", ssc)])
            for tt in range(2):
                mmr = p.add("tensor", lambda t, c=c, tt=tt: t.matmul(
                    p.P2[0:1, tt * 512:(tt + 1) * 512], p.ones_c[:], sqA[0:128, tt * 512:(tt + 1) * 512],
                    start=(c == 0), stop=(c == nch - 1)),
                    inc="mm" if tt == 1 else None)
            sq_marks[c] = mmr
        f = p.add("vector", lambda v: v.tensor_scalar(r0, p.P2[0:1, 0:1024], inv_n, None, ALU.mult),
                  waits=[("mm", mmr)], inc="vf")
        fc = p.c("vf")
        f = p.add("vector", lambda v: v.tensor_scalar(r0, r0, float(EPS_RMS), None, ALU.add),
                  waits=[("vf", fc)], inc="ve")
        s1 = p.add("scalar", lambda s: s.activation(out=r1, in_=r0, func=AF.Sqrt),
                   waits=[("ve", f)], inc="sc")
        f2 = p.add("vector", lambda v: v.reciprocal(r1, r1), waits=[("sc", s1)], inc="ve")
        bm = None
        for tt in range(2):
            bm = p.add("tensor", lambda t, tt=tt: t.matmul(
                p.P3[:, tt * 512:(tt + 1) * 512], p.ones_rf[:], r1[0:1, tt * 512:(tt + 1) * 512],
                start=True, stop=True),
                waits=[("ve", f2)] if tt == 0 else (), inc="mm" if tt == 1 else None)
        f3 = p.add("vector", lambda v: v.tensor_copy(rb1[:], p.P3[:]), waits=[("mm", bm)], inc="ve")
        outm = None
        for c in range(nch):
            p.add("vector", lambda v, c=c: v.tensor_tensor(src_buf[:, c, :], src_buf[:, c, :], rb1[:],
                                                           ALU.mult),
                  waits=[("ve", f3)] if c == 0 else (), inc="vf")
            fcc = p.c("vf")
            outm = p.add("vector", lambda v, c=c: v.tensor_scalar(
                out_buf[:, c, :], src_buf[:, c, :], gain[0:128, c:c + 1], None, ALU.mult),
                waits=[("vf", fcc)], inc="ve")
        return outm

    qa_done = em[11][1]
    qan_ve = emit_rms(qa, 6, 1.0 / 768.0, p.qgain, q_an, qa_done)
    ckv_done = em[grp - 1][1]
    kvn_ve = emit_rms(ckv, 4, 1.0 / 512.0, p.kvgain, kvn_loc, ckv_done,
                      dep_extra=[("sc", krb_sc)])

    p.add("tensor", lambda t: t.ldweights(p.rt[:]), waits=[("sc", krb_sc), ("ve", kvn_ve)])
    rot_mm = None
    for tt in range(2):
        rot_mm = p.add("tensor", lambda t, tt=tt: t.matmul(
            [p.P2[0:64, 0:512], p.P2[0:64, 512:1024]][tt], p.rt[:], krb[0:64, tt * 512:(tt + 1) * 512],
            start=True, stop=True), inc="mm" if tt == 1 else None)
    f = p.add("vector", lambda v: v.tensor_tensor(kroped_loc[:], krb[:], p.trig[0:64, 0:T], ALU.mult),
              waits=[("mm", rot_mm)], inc="vf")
    fc = p.c("vf")
    f = p.add("vector", lambda v: v.tensor_tensor(krb[:], p.P2[0:64, :], p.trig[0:64, T:2 * T], ALU.mult),
              waits=[("vf", fc)], inc="vf")
    fc = p.c("vf")
    krp_ve = p.add("vector", lambda v: v.tensor_tensor(kroped_loc[:], kroped_loc[:], krb[:], ALU.add),
                   waits=[("vf", fc)], inc="ve")

    pm1 = p.dma("sync", lambda e: e.dma_start(
        out=cin[0:512, :].rearrange("(c p) t -> p c t", p=128), in_=kvn_loc[:]),
        waits=[("ve", kvn_ve)])
    pm2 = p.dma("sync", lambda e: e.dma_start(out=cin[512:576, :], in_=kroped_loc[:]),
                waits=[("ve", krp_ve)])
    p.add("gpsimd", lambda g: g.collective_compute(
        "AllGather", ALU.bypass, replica_groups=[[0, 1], [2, 3], [4, 5], [6, 7]],
        ins=[cin.tensor.ap().opt()], outs=[cout.tensor.ap().opt()]),
        waits=[pm1, pm2], inc="cc")
    cc_mark = ("cc", p.c("cc"))

    kvnf = arena_v(p, 0, [128, 4, S], BF16)
    kropedf = arena_v(p, 16384, [64, S], BF16)
    krn2 = arena_v(p, 20480, [1, S], BF16)
    sqB = arena_v(p, 28672, [128, S], BF16)
    wqbs = [arena_v(p, 32768 + i * 2304, [128, 6, 192], BF16) for i in range(2)]
    wkvbs = [arena_v(p, 37376 + i * 2048, [128, 4, 256], BF16) for i in range(2)]
    qnh = [arena_v(p, 41472 + i * 2048, [128, T], BF16) for i in range(2)]
    qrh = [arena_v(p, 45568 + i * 2048, [64, T], BF16) for i in range(2)]
    t1b = arena_v(p, 49664, [64, T], BF16)
    khn = [arena_v(p, 51712 + i * 4096, [128, S], BF16) for i in range(2)]
    vvt = [arena_v(p, 59904 + i * 4096, [128, NKT, 128], BF16) for i in range(2)]
    mskb = [arena_v(p, 68096 + i * 2048, [128, T], BF16) for i in range(2)]
    Pt = [arena_v(p, 72192 + i * 2048, [128, T], BF16) for i in range(3)]
    rbb = arena_v(p, 24576, [128, T], F32)
    O = p.xn

    gb = [cc_mark, ("ve", p.c("ve")), ("sc", p.c("sc")), ("mm", p.c("mm"))]
    gbms = []
    for r in range(2):
        gbms.append(p.dma("sync", lambda e, r=r: e.dma_start(
            out=kvnf[:, :, r * 1024:(r + 1) * 1024],
            in_=cout[r * CIN1_ROWS:r * CIN1_ROWS + 512, :].rearrange("(c p) t -> p c t", p=128)),
            waits=gb if r == 0 else ()))
        gbms.append(p.dma("sync", lambda e, r=r: e.dma_start(
            out=kropedf[:, r * 1024:(r + 1) * 1024],
            in_=cout[r * CIN1_ROWS + 512:r * CIN1_ROWS + 576, :])))
    gbld = gbms

    kq_sc = p.add("scalar", lambda s: s.activation(out=sqB[0:64, 0:S], in_=kropedf[:], func=AF.Square),
                  waits=gbld, inc="sc")
    p.add("tensor", lambda t: t.ldweights(p.ones_c[0:64, :]), waits=[("sc", kq_sc)])
    krn_mm = {}
    krn_cp = {}
    for half in range(2):
        for tt in range(2):
            krn_mm[half] = p.add("tensor", lambda t, half=half, tt=tt: t.matmul(
                p.P2[0:1, tt * 512:(tt + 1) * 512], p.ones_c[0:64, :],
                sqB[0:64, half * 1024 + tt * 512: half * 1024 + (tt + 1) * 512],
                start=True, stop=True),
                waits=[("ve", krn_cp[half - 1])] if (half == 1 and tt == 0) else (),
                inc="mm" if tt == 1 else None)
        krn_cp[half] = p.add("vector", lambda v, half=half: v.tensor_copy(
            krn2[0:1, half * 1024:(half + 1) * 1024], p.P2[0:1, 0:1024]),
            waits=[("mm", krn_mm[half])], inc="ve")
    krn_ve = ("ve", p.c("ve"))

    psP = [p.P0, p.P1]
    gm = [0]
    msk_ve = {}
    jit_marks = {}
    head_end_mm = {}
    o_ev = {}
    rbcopy_f = {}
    prev_exp = {}
    mmo_kt = {}
    wld_mark = {}

    def head_wload(h):
        hw = []
        if h >= 2:
            hw = [jit_marks[(h - 2, "jit_end")]]
        mk1 = p.dma("gpsimd", lambda g, h=h: g.dma_start(
            out=wqbs[h % 2][:],
            in_=wqb_d.rearrange("(c p) f -> p c f", p=128)[:, :, 192 * h:192 * (h + 1)]),
            waits=hw)
        mk2 = p.dma("gpsimd", lambda g, h=h: g.dma_start(
            out=wkvbs[h % 2][:],
            in_=wkvb_d.rearrange("(c p) f -> p c f", p=128)[:, :, 256 * h:256 * (h + 1)]))
        wld_mark[h] = [mk1, mk2]

    def emit_jit(h, first):
        hb = h % 2
        pg = list(wld_mark[h]) + list(gbld) + [("ve", qan_ve), krn_ve]
        if not first:
            pg.append(("sc", prev_exp[(h - 1, NKT - 1)]))
        qn_mm = {}
        for tt in range(2):
            for qc in range(6):
                p.add("tensor", lambda t, h=h, hb=hb, qc=qc: t.ldweights(wqbs[hb][:, qc, 0:128]),
                      waits=pg if (tt == 0 and qc == 0) else ())
                qn_mm[tt] = p.add("tensor", lambda t, h=h, hb=hb, qc=qc, tt=tt: t.matmul(
                    p.P0[:, tt * 512:(tt + 1) * 512], wqbs[hb][:, qc, 0:128],
                    q_an[:, qc, tt * 512:(tt + 1) * 512],
                    start=(qc == 0), stop=(qc == 5)),
                    inc="mm" if qc == 5 else None)
        qn_ev = p.add("scalar", lambda s, hb=hb: s.copy(out=qnh[hb][:], in_=p.P0[:]),
                      waits=[("mm", qn_mm[1])], inc="sc")
        qr_mm = {}
        for tt in range(2):
            for qc in range(6):
                p.add("tensor", lambda t, hb=hb, qc=qc: t.ldweights(wqbs[hb][:, qc, 128:192]))
                qr_mm[tt] = p.add("tensor", lambda t, hb=hb, qc=qc, tt=tt: t.matmul(
                    p.P1[0:64, tt * 512:(tt + 1) * 512], wqbs[hb][:, qc, 128:192],
                    q_an[:, qc, tt * 512:(tt + 1) * 512],
                    start=(qc == 0), stop=(qc == 5)),
                    inc="mm" if qc == 5 else None)
        qr_ev = p.add("scalar", lambda s, hb=hb: s.copy(out=qrh[hb][:], in_=p.P1[0:64, :]),
                      waits=[("mm", qr_mm[1])], inc="sc")
        p.add("tensor", lambda t: t.ldweights(p.rt[:]), waits=[("sc", qr_ev)])
        rot = None
        for tt in range(2):
            rot = p.add("tensor", lambda t, hb=hb, tt=tt: t.matmul(
                p.P1[0:64, tt * 512:(tt + 1) * 512], p.rt[:], qrh[hb][0:64, tt * 512:(tt + 1) * 512],
                start=True, stop=True), inc="mm" if tt == 1 else None)
        f = p.add("vector", lambda v, hb=hb: v.tensor_tensor(t1b[:], qrh[hb][:], p.trig[0:64, 0:T],
                                                             ALU.mult),
                  waits=[("mm", rot)], inc="vf")
        fcv = p.c("vf")
        f = p.add("vector", lambda v, hb=hb: v.tensor_tensor(qrh[hb][:], p.P1[0:64, :],
                                                             p.trig[0:64, T:2 * T], ALU.mult),
                  waits=[("vf", fcv)], inc="vf")
        fcv = p.c("vf")
        rope_p1_f = fcv
        qr_rope_ve = p.add("vector", lambda v, hb=hb: v.tensor_tensor(qrh[hb][:], t1b[:], qrh[hb][:],
                                                                      ALU.add),
                           waits=[("vf", fcv)], inc="ve")
        kn_mm = {}
        for nt in range(4):
            bank = [p.P0[:, 0:512], p.P0[:, 512:1024]][nt % 2]
            kw = [("sc", qn_ev)]
            if nt >= 2:
                kw.append(("sc", kn_mm[nt - 2][1]))
            for kc in range(4):
                p.add("tensor", lambda t, hb=hb, kc=kc: t.ldweights(wkvbs[hb][:, kc, 0:128]),
                      waits=kw if kc == 0 else ())
                mmv = p.add("tensor", lambda t, hb=hb, kc=kc, nt=nt, bank=bank: t.matmul(
                    bank, wkvbs[hb][:, kc, 0:128], kvnf[:, kc, nt * 512:(nt + 1) * 512],
                    start=(kc == 0), stop=(kc == 3)),
                    inc="mm" if kc == 3 else None)
            ev = p.add("scalar", lambda s, hb=hb, nt=nt, bank=bank: s.copy(
                out=khn[hb][:, nt * 512:(nt + 1) * 512], in_=bank),
                waits=[("mm", mmv)], inc="sc")
            kn_mm[nt] = (mmv, ev)
        vv_ev = {}
        for bank_i in range(4):
            bank = [p.P1[:, 0:512], p.P1[:, 512:1024]][bank_i % 2]
            bw_ = [("mm", rot), ("vf", rope_p1_f)]
            if bank_i >= 2:
                bw_.append(("sc", vv_ev[bank_i - 2]))
            first_mm = True
            for sl in range(4):
                kt = bank_i * 4 + sl
                for kc in range(4):
                    p.add("tensor", lambda t, hb=hb, kc=kc: t.ldweights(wkvbs[hb][:, kc, 128:256]),
                          waits=bw_ if (sl == 0 and kc == 0) else ())
                    mmv = p.add("tensor", lambda t, hb=hb, kc=kc, kt=kt, sl=sl, bank=bank,
                                first_mm=first_mm: t.matmul(
                        bank[0:128, sl * 128:(sl + 1) * 128],
                        kvnf[:, kc, kt * 128:(kt + 1) * 128], wkvbs[hb][:, kc, 128:256],
                        start=first_mm, stop=(kc == 3 and sl == 3)),
                        inc="mm" if (kc == 3 and sl == 3) else None)
                    first_mm = False
            vv_ev[bank_i] = p.add("scalar", lambda s, hb=hb, bank_i=bank_i, bank=bank: s.copy(
                out=vvt[hb][:, bank_i * 4:(bank_i + 1) * 4, :].rearrange("p a b -> p (a b)"),
                in_=bank),
                waits=[("mm", mmv)], inc="sc")
        sq1 = p.add("scalar", lambda s, hb=hb: s.activation(out=sqB[0:128, 0:T], in_=qnh[hb][:],
                                                            func=AF.Square),
                    waits=[("sc", kn_mm[3][1]), ("ve", qr_rope_ve)], inc="sc")
        p.add("tensor", lambda t: t.ldweights(p.ones_c[:]),
              waits=[("sc", sq1), ("sc", kn_mm[3][1])])
        qs1_mm = None
        for tt in range(2):
            qs1_mm = p.add("tensor", lambda t, tt=tt: t.matmul(
                p.P0[0:1, tt * 512:(tt + 1) * 512], p.ones_c[:], sqB[0:128, tt * 512:(tt + 1) * 512],
                start=True, stop=False), inc="mm" if tt == 1 else None)
        sq2 = p.add("scalar", lambda s, hb=hb: s.activation(out=sqB[0:64, 0:T], in_=qrh[hb][:],
                                                            func=AF.Square),
                    waits=[("mm", qs1_mm)], inc="sc")
        p.add("tensor", lambda t: t.ldweights(p.ones_c[0:64, :]), waits=[("sc", sq2)])
        qs_mm = None
        for tt in range(2):
            qs_mm = p.add("tensor", lambda t, tt=tt: t.matmul(
                p.P0[0:1, tt * 512:(tt + 1) * 512], p.ones_c[0:64, :], sqB[0:64, tt * 512:(tt + 1) * 512],
                start=False, stop=True), inc="mm" if tt == 1 else None)
        mq = p.add("vector", lambda v: v.reduce_max(r2[0:1, 0:1], p.P0[0:1, 0:1024], axis=AX.X),
                   waits=[("mm", qs_mm)], inc="vf")
        kb_mark = None
        for half in range(2):
            sq3 = p.add("scalar", lambda s, hb=hb, half=half: s.activation(
                out=sqB[0:128, 0:T], in_=khn[hb][:, half * 1024:(half + 1) * 1024], func=AF.Square),
                waits=[("vf", mq)] if half == 0 else [("vf", kb_mark)], inc="sc")
            p.add("tensor", lambda t: t.ldweights(p.ones_c[:]), waits=[("sc", sq3)])
            ks_mm = None
            for tt in range(2):
                ks_mm = p.add("tensor", lambda t, tt=tt: t.matmul(
                    p.P0[0:1, tt * 512:(tt + 1) * 512], p.ones_c[:], sqB[0:128, tt * 512:(tt + 1) * 512],
                    start=True, stop=True), inc="mm" if tt == 1 else None)
            f = p.add("vector", lambda v, half=half: v.tensor_tensor(
                r1[0:1, 0:1024], p.P0[0:1, 0:1024], krn2[0:1, half * 1024:(half + 1) * 1024], ALU.add),
                waits=[("mm", ks_mm)], inc="vf")
            fcv = p.c("vf")
            kb_mark = p.add("vector", lambda v, half=half: v.reduce_max(
                r2[0:1, 1 + half:2 + half], r1[0:1, 0:1024], axis=AX.X),
                waits=[("vf", fcv)], inc="vf")
        f = p.add("vector", lambda v: v.tensor_tensor(r2[0:1, 1:2], r2[0:1, 1:2], r2[0:1, 2:3], ALU.max),
                  waits=[("vf", kb_mark)], inc="vf")
        fcv = p.c("vf")
        f = p.add("vector", lambda v: v.tensor_tensor(r2[0:1, 0:1], r2[0:1, 0:1], r2[0:1, 1:2], ALU.mult),
                  waits=[("vf", fcv)], inc="ve")
        sqb = p.add("scalar", lambda s, h=h: s.activation(out=mb1[0:1, h:h + 1], in_=r2[0:1, 0:1],
                                                          func=AF.Sqrt, scale=float(SCALE1 * SCALE1)),
                    waits=[("ve", f)], inc="sc")
        bias_mm = p.add("tensor", lambda t, h=h: t.matmul(
            p.P0[:, 0:1], p.neg_rb[:], mb1[0:1, h:h + 1], start=True, stop=True),
            waits=[("sc", sqb)], inc="mm")
        bv = p.add("vector", lambda v, h=h: v.tensor_copy(p.biases[0:128, h:h + 1], p.P0[:, 0:1]),
                   waits=[("mm", bias_mm)], inc="ve")
        jit_marks[(h, "bias")] = ("ve", bv)
        jit_marks[(h, "qr")] = ("ve", qr_rope_ve)
        jit_marks[(h, "qn")] = ("sc", qn_ev)
        jit_marks[(h, "kn")] = ("sc", kn_mm[3][1])
        jit_marks[(h, "vv")] = ("sc", vv_ev[3])
        jit_marks[(h, "jit_end")] = ("mm", p.c("mm"))

    head_wload(0)
    head_wload(1)
    emit_jit(0, True)
    for h in range(H):
        hb = h % 2
        exp_sc = {}
        for j in range(NKT + 2):
            if j < NKT:
                g = gm[0]
                mw = [("ve", msk_ve[g - 2])] if g >= 2 else []
                msk_sd = p.dma("scalar", lambda sc, j=j, g=g: sc.dma_start(
                    out=mskb[g % 2][:], in_=mask_d[j * 128:(j + 1) * 128, :]),
                    waits=mw)
                pw = [jit_marks[(h, "bias")], jit_marks[(h, "qr")], jit_marks[(h, "kn")],
                      jit_marks[(h, "vv")]]
                if j >= 2:
                    pw.append(("sc", exp_sc[j - 2]))
                elif h > 0:
                    pw.append(("sc", prev_exp[(h - 1, NKT - 2 + j)]))
                p.add("tensor", lambda t, j=j, hb=hb: t.ldweights(khn[hb][:, j * 128:(j + 1) * 128]),
                      waits=pw)
                for qt in range(2):
                    p.add("tensor", lambda t, j=j, qt=qt, hb=hb: t.matmul(
                        psP[j % 2][:, qt * 512:(qt + 1) * 512], khn[hb][:, j * 128:(j + 1) * 128],
                        qnh[hb][0:128, qt * 512:(qt + 1) * 512], start=True, stop=False))
                p.add("tensor", lambda t, j=j: t.ldweights(kropedf[0:64, j * 128:(j + 1) * 128]))
                mmv = None
                for qt in range(2):
                    mmv = p.add("tensor", lambda t, j=j, qt=qt, hb=hb: t.matmul(
                        psP[j % 2][:, qt * 512:(qt + 1) * 512], kropedf[0:64, j * 128:(j + 1) * 128],
                        qrh[hb][0:64, qt * 512:(qt + 1) * 512], start=False, stop=True),
                        inc="mm" if qt == 1 else None)
                ew = [("mm", mmv)]
                if j >= 3:
                    ew.append(("mm", mmo_kt[(h, j - 3)]))
                elif h > 0:
                    ew.append(("mm", mmo_kt[(h - 1, NKT - 3 + j)]))
                exp_sc[j] = p.add("scalar", lambda s, h=h, j=j: s.activation(
                    out=Pt[j % 3][:], in_=psP[j % 2][:], func=AF.Exp,
                    bias=p.biases[0:128, h:h + 1], scale=float(SCALE1)),
                    waits=ew, inc="sc")
                prev_exp[(h, j)] = exp_sc[j]
                msk_ve[g] = p.add("vector", lambda v, j=j, g=g: v.tensor_tensor(
                    Pt[j % 3][:], Pt[j % 3][:], mskb[g % 2][:], ALU.mult),
                    waits=[("sc", exp_sc[j]), msk_sd], inc="ve")
                gm[0] += 1
            if j >= 2:
                kt = j - 2
                sw = [("ve", msk_ve[h * NKT + kt])]
                if kt == 0 and h > 0:
                    sw.append(("vf", rbcopy_f[h - 1]))
                    sw.append(("ve", o_ev[h - 1]))
                p.add("tensor", lambda t: t.ldweights(p.ones_c[:]), waits=sw)
                for qt in range(2):
                    p.add("tensor", lambda t, kt=kt, qt=qt: t.matmul(
                        p.P2[0:1, qt * 512:(qt + 1) * 512], p.ones_c[:],
                        Pt[kt % 3][0:128, qt * 512:(qt + 1) * 512],
                        start=(kt == 0), stop=(kt == NKT - 1)))
                p.add("tensor", lambda t, kt=kt, hb=hb: t.ldweights(vvt[hb][:, kt, :]))
                mmo = None
                for qt in range(2):
                    mmo = p.add("tensor", lambda t, kt=kt, qt=qt, hb=hb: t.matmul(
                        p.P3[:, qt * 512:(qt + 1) * 512], vvt[hb][:, kt, :],
                        Pt[kt % 3][0:128, qt * 512:(qt + 1) * 512],
                        start=(kt == 0), stop=(kt == NKT - 1)),
                        inc="mm" if qt == 1 else None)
                mmo_kt[(h, kt)] = mmo
        head_end_mm[h] = ("mm", mmo_kt[(h, NKT - 1)])
        f = p.add("vector", lambda v: v.tensor_scalar(r1, p.P2[0:1, 0:1024], 1e-30, None, ALU.max),
                  waits=[head_end_mm[h]], inc="vf")
        f2 = p.add("vector", lambda v: v.reciprocal(r2, r1), waits=[("vf", f)], inc="ve")
        rb_mm = None
        for qt in range(2):
            rb_mm = p.add("tensor", lambda t, qt=qt: t.matmul(
                p.P2[:, qt * 512:(qt + 1) * 512], p.ones_rf[:], r2[0:1, qt * 512:(qt + 1) * 512],
                start=True, stop=True),
                waits=[("ve", f2)] if qt == 0 else (), inc="mm" if qt == 1 else None)
        if h + 1 < H:
            if h + 2 < H:
                head_wload(h + 2)
            emit_jit(h + 1, False)
        f3 = p.add("vector", lambda v: v.tensor_copy(rbb[:], p.P2[:]), waits=[("mm", rb_mm)], inc="vf")
        rbcopy_f[h] = f3
        o_ev[h] = p.add("vector", lambda v, h=h: v.tensor_tensor(
            O[:, h, :], p.P3[:], rbb[:], ALU.mult),
            waits=[("vf", f3)], inc="ve")

    wslO = [arena_v(p, i * 8192, [128, DC, 256], BF16) for i in range(2)]
    PQ2 = [p.P0[:, 0:512], p.P0[:, 512:1024], p.P1[:, 0:512], p.P1[:, 512:1024]]
    grp2 = 0
    emo = {}
    pjo = {}
    for s in range(8):
        guard = [head_end_mm[H - 1], ("ve", o_ev[H - 1])] if s < 2 else [pjo[(s - 2, 3)]]
        wl = wload(wo_d, s, wslO[s % 2][:], guard)
        for sub in range(2):
            fc2 = 2 * s + sub
            for tt in range(2):
                g4 = grp2 % 4
                gwaits = [wl, ("ve", o_ev[H - 1])]
                if grp2 >= 4:
                    gwaits.append(emo[grp2 - 4])
                for fc in range(DC):
                    p.add("tensor", lambda t, s=s, fc=fc, sub=sub: t.ldweights(
                        wslO[s % 2][:, fc, sub * 128:(sub + 1) * 128]),
                        waits=gwaits if fc == 0 else ())
                    mmv = p.add("tensor", lambda t, s=s, fc=fc, sub=sub, tt=tt, g4=g4: t.matmul(
                        PQ2[g4], wslO[s % 2][:, fc, sub * 128:(sub + 1) * 128],
                        O[:, fc, tt * 512:(tt + 1) * 512],
                        start=(fc == 0), stop=(fc == DC - 1)),
                        inc="mm" if fc == DC - 1 else None)
                pjo[(s, 2 * sub + tt)] = ("mm", mmv)
                ev = p.add("vector", lambda v, fc2=fc2, tt=tt, g4=g4: v.tensor_tensor(
                    p.xs[:, fc2, tt * 512:(tt + 1) * 512], p.xs[:, fc2, tt * 512:(tt + 1) * 512],
                    PQ2[g4], ALU.add),
                    waits=[("mm", mmv)], inc="ve")
                emo[grp2] = ("ve", ev)
                grp2 += 1



def build_program(phases):
    p = P()
    setup_common(p, set(phases))
    load_x(p)
    for ph in phases:
        if ph == "ffn0":
            emit_ffn0(p)
        elif ph == "moe":
            emit_moe(p)
        elif ph == "attn0":
            emit_attn0(p)
        elif ph == "attn1":
            emit_attn1(p)
        else:
            raise NotImplementedError(ph)
    store_x(p)
    nc = p.build()
    return p, nc


def prep_static(inputs):
    g = {}
    g["wT"] = np.ascontiguousarray(inputs["L0_ffn_weight"].T)
    g["rT"] = np.ascontiguousarray(inputs["L0_ffn_rram_variation"].T).astype(BF)
    g["gwT"] = np.ascontiguousarray(inputs["L1_ffn_gate_w"].T)
    g["w1T"] = np.ascontiguousarray(np.asarray(inputs["L1_ffn_w1"]).transpose(0, 2, 1))
    g["w3T"] = np.ascontiguousarray(np.asarray(inputs["L1_ffn_w3"]).transpose(0, 2, 1))
    g["w2T"] = np.ascontiguousarray(np.asarray(inputs["L1_ffn_w2"]).transpose(0, 2, 1))
    g["sw1T"] = np.ascontiguousarray(inputs["L1_ffn_sw1"].T)
    g["sw3T"] = np.ascontiguousarray(inputs["L1_ffn_sw3"].T)
    g["sw2T"] = np.ascontiguousarray(inputs["L1_ffn_sw2"].T)
    g["idm"] = np.eye(128, dtype=np.float32)
    sel = np.zeros((NE, NE * 128), dtype=np.float32)
    for e in range(NE):
        sel[e, e * 128:(e + 1) * 128] = 0.5
    g["sel"] = sel.astype(BF)
    return g


def per_core_arrays(c):
    half = c % 2
    qg = half * T + np.arange(T)
    kg = np.arange(S)
    keep = (kg[:, None] > qg[None, :])
    keep[:, qg == 2047] = True
    out = {}
    out["mask0"] = keep.astype(np.float32).astype(BF)
    out["qmrow"] = (qg != 2047).astype(np.float32).reshape(1, T)
    keep1 = (kg[:, None] <= qg[None, :])
    out["mask1"] = keep1.astype(np.float32).astype(BF)
    pos = (half * T + np.arange(T)).astype(np.float32)
    inv = 1.0 / (10000.0 ** (np.arange(0, 64, 2, dtype=np.float32) / 64.0))
    fr = np.outer(pos, inv)
    emb = np.concatenate([fr, fr], -1)
    trig = np.zeros((64, 2 * T), dtype=np.float32)
    trig[:, 0:T] = np.cos(emb).T
    trig[:, T:2 * T] = np.sin(emb).T
    out["trig"] = trig.astype(BF)
    return out


_RT = np.zeros((64, 64), dtype=np.float32)
for _i in range(32):
    _RT[2 * _i + 1, 2 * _i] = -1.0
    _RT[2 * _i, 2 * _i + 1] = 1.0


def run_phases(phases, x_full, raw_inputs):
    return _run_launch(phases, x_full, _full_g(raw_inputs))


_PROG_CACHE = {}


def _get_program(phases):
    key = tuple(phases)
    if key not in _PROG_CACHE:
        _PROG_CACHE[key] = build_program(list(phases))
    return _PROG_CACHE[key]


MACARON = ["ffn0", "attn0", "ffn0", "moe", "attn1", "moe"]
FUSED = False


def _run_launch(phases, x_full, g):
    p, nc = _get_program(phases)
    in_maps = []
    for c in range(8):
        b, half = c // 2, c % 2
        pc = per_core_arrays(c)
        im = {}
        for name in p.inputs:
            if name == "x":
                im["x"] = np.ascontiguousarray(x_full[b, half * T:(half + 1) * T, :].T)
            elif name in pc:
                im[name] = pc[name]
            elif name in g:
                im[name] = g[name]
            else:
                raise KeyError(name)
        in_maps.append(im)
    res = run_bass_kernel_spmd(nc, in_maps, core_ids=list(range(8)))
    out = np.empty((4, S, D), dtype=np.float32)
    for c in range(8):
        b, half = c // 2, c % 2
        out[b, half * T:(half + 1) * T, :] = res.results[c]["y"].T
    return out


def _full_g(raw_inputs):
    g = prep_static(raw_inputs)
    g["rt"] = _RT.astype(BF)
    g["qgain"] = np.ascontiguousarray(np.asarray(raw_inputs["L1_attn_q_a_norm_w"]).reshape(6, 128).T)
    g["kvgain"] = np.ascontiguousarray(np.asarray(raw_inputs["L1_attn_kv_a_norm_w"]).reshape(4, 128).T)
    for k, nm in [("Wq", "L0_attn_Wq"), ("Wk", "L0_attn_Wk"), ("Wv", "L0_attn_Wv"),
                  ("Wo", "L0_attn_Wo"), ("w_qa", "L1_attn_w_qa"), ("w_qb", "L1_attn_w_qb"),
                  ("w_kva", "L1_attn_w_kva"), ("w_kvb", "L1_attn_w_kvb"), ("w_o", "L1_attn_w_o")]:
        g[k] = np.asarray(raw_inputs[nm], dtype=np.float32)
    return g


def kernel(**inputs):
    x = np.asarray(inputs["L0_attn_x"], dtype=np.float32)
    g = _full_g(inputs)
    if FUSED:
        return _run_launch(MACARON, x, g)
    for ph in MACARON:
        x = _run_launch([ph], x, g)
    return x
